# version 01
# baseline (speedup 1.0000x reference)
import numpy as np
from contextlib import ExitStack

import concourse.bass as bass
import concourse.mybir as mybir
from concourse.bass_utils import run_bass_kernel_spmd

F32 = mybir.dt.float32
BF16 = mybir.dt.bfloat16
U8 = mybir.dt.uint8
AF = mybir.ActivationFunctionType
ALU = mybir.AluOpType
AX = mybir.AxisListType.X

D = 4096
NCF = 32
NT = 1024
NTT = 8
HQ = 32
HKV = 8
LN_EPS = 1e-5
ATT_EPS = 1e-10

CHUNKS = []
for oc in range(HKV):
    CHUNKS.append(("wk", oc))
    CHUNKS.append(("wv", oc))
for j in range(HQ):
    CHUNKS.append(("wq", j))
CHUNKS.append(("apack", 0))
CHUNKS.append(("apack", 1))
CHUNKS.append(("bft", 0))
CHUNKS.append(("bft", 1))
for j in range(32):
    CHUNKS.append(("wo", j))
    CHUNKS.append(("wf", j))
NCHUNK = len(CHUNKS)

N_LAND = 2
N_RING = 2


def build_nc():
    nc = bass.Bass("TRN2", target_bir_lowering=False, debug=False)

    din = {}
    for name, shape in [
        ("x_tok", [NT, D]), ("x_fm", [D, NT]),
        ("wq", [32, 128, 4096]), ("wk", [8, 128, 4096]), ("wv", [8, 128, 4096]),
        ("wo", [32, 128, 4096]), ("wf", [32, 128, 4096]),
        ("apack", [2, 128, 4096]), ("bft", [2, 128, 4096]),
        ("bq", [128, 32]), ("bk", [128, 8]), ("bv", [128, 8]),
        ("bo", [128, 32]), ("bfb", [128, 32]),
        ("creps", [128, 2]), ("sreps", [128, 2]),
        ("ident", [128, 128]),
    ]:
        din[name] = nc.dram_tensor(name, shape, F32, kind="ExternalInput").ap()
    out_fm = nc.dram_tensor("out_fm", [D, NT], F32, kind="ExternalOutput").ap()
    cc_in = nc.dram_tensor("cc_in", [128, 1032], F32).ap()
    cc_out = nc.dram_tensor("cc_out", [128, 1032], F32).ap()

    with ExitStack() as ctx:
        def sb(name, shape, dt):
            return ctx.enter_context(nc.sbuf_tensor(name, shape, dt))

        def ps(name, shape, dt):
            return ctx.enter_context(nc.psum_tensor(name, shape, dt))

        xn_fm = sb("xn_fm", [128, NCF * 1024], BF16)
        ring = sb("ring", [128, N_RING * 4096], BF16)
        land = sb("land", [128, N_LAND * 2048], F32)
        bft_w = sb("bft_w", [128, 2 * 4096], BF16)
        kvt_bf = sb("kvt_bf", [128, 1024], BF16)
        staging = sb("staging", [128, 1032], F32)
        stats = sb("stats", [128, 32], F32)
        biases = sb("biases", [128, 32 * 3 + 8 * 2 + 2 * 3], F32)
        ident_f = sb("ident_f", [128, 128], F32)
        ident_b = sb("ident_b", [128, 128], BF16)
        ones_c = sb("ones_c", [128, 1], BF16)
        q_bf = sb("q_bf", [128, 2 * 512], BF16)
        xa_bf = sb("xa_bf", [128, 2 * 1024], BF16)
        e_scr = sb("e_scr", [128, 512], F32)
        r_scr = sb("r_scr", [128, 512], F32)
        s_scr = sb("s_scr", [128, 512], F32)
        g_scr = sb("g_scr", [128, 512], F32)
        x_scr = sb("x_scr", [128, 3 * 512], F32)
        o_scr = sb("o_scr", [128, 2 * 512], F32)
        arena = sb("arena", [128, 65536], U8)

        xt_sl = [arena[:, 0:16384].bitcast(F32), arena[:, 16384:32768].bitcast(F32)]
        xn_tm = arena[:, 32768:40960].bitcast(BF16)
        k_fm = arena[:, 40960:43008].bitcast(BF16)
        v_fm = arena[:, 43008:45056].bitcast(BF16)
        k_tm = arena[:, 45056:47104].bitcast(BF16)
        v_tm = arena[:, 47104:49152].bitcast(BF16)
        attn_bf = arena[:, 0:65536].bitcast(BF16)

        bq_c = biases[:, 0:32]
        bo_c = biases[:, 32:64]
        bfb_c = biases[:, 64:96]
        bk_c = biases[:, 96:104]
        bv_c = biases[:, 104:112]
        creps_c = biases[:, 112:114]
        sreps_c = biases[:, 114:116]
        w128_c = biases[:, 116:118]

        PT = ps("pt", [128, 1024], BF16)
        P0 = ps("p0", [128, 512], F32)
        P1 = ps("p1", [128, 512], F32)
        P2 = ps("p2", [128, 512], F32)
        P3 = ps("p3", [128, 512], F32)
        P4 = ps("p4", [128, 512], F32)

        Sg = ctx.enter_context(nc.semaphore("sg"))
        Swq = ctx.enter_context(nc.semaphore("swq"))
        Sxq = ctx.enter_context(nc.semaphore("sxq"))
        Sod = ctx.enter_context(nc.semaphore("sod"))
        Scc = ctx.enter_context(nc.semaphore("scc"))

        block = ctx.enter_context(nc.Block())

        streams = {"sync": [], "tensor": [], "vector": [], "scalar": [],
                   "gpsimd": []}
        C = {"g": 0, "wq": 0, "xq": 0, "od": 0}

        def em(eng, fn):
            streams[eng].append(fn)

        def gstep(eng, builder, inc=1, wq=0, xq=0, od=0, gv=None):
            wv = C["g"] if gv is None else gv
            def fn(e, wv=wv, wq=wq, xq=xq, od=od, builder=builder, inc=inc):
                if wq:
                    e.wait_ge(Swq, wq)
                if xq:
                    e.wait_ge(Sxq, xq)
                if od:
                    e.wait_ge(Sod, od)
                if wv:
                    e.wait_ge(Sg, wv)
                inst = builder(e)
                if inc:
                    inst.then_inc(Sg, inc)
            em(eng, fn)
            C["g"] += inc
            return C["g"]

        land_free_g = {}
        half_wq_target = {}
        chunk_cast_g = {}
        issued_halves = [0]

        def chunk_src(ci):
            kind, idx = CHUNKS[ci]
            return din[kind][idx]

        def issue_half(hi):
            if hi >= NCHUNK * 2 or hi in half_wq_target:
                return
            assert hi == issued_halves[0], (hi, issued_halves[0])
            ci, part = hi // 2, hi % 2
            src = chunk_src(ci)[:, part * 2048:(part + 1) * 2048]
            dst = land[:, (hi % N_LAND) * 2048:((hi % N_LAND) + 1) * 2048]
            if hi >= N_LAND:
                gv = land_free_g[hi - N_LAND]
            else:
                gv = 0
            C["wq"] += 16
            tgt = C["wq"]
            def fn(e, src=src, dst=dst, gv=gv):
                if gv:
                    e.wait_ge(Sg, gv)
                e.dma_start(out=dst, in_=src).then_inc(Swq, 16)
            em("sync", fn)
            half_wq_target[hi] = tgt
            issued_halves[0] += 1

        def cast_chunk(ci):
            kind, idx = CHUNKS[ci]
            for part in range(2):
                hi = 2 * ci + part
                lsl = land[:, (hi % N_LAND) * 2048:((hi % N_LAND) + 1) * 2048]
                if kind == "bft":
                    dst = bft_w[:, idx * 4096 + part * 2048: idx * 4096 + (part + 1) * 2048]
                    wcol = w128_c[:, idx:idx + 1]
                    def bl(e, dst=dst, lsl=lsl, wcol=wcol):
                        return e.tensor_scalar_mul(dst, lsl, wcol)
                    gstep("vector", bl, wq=half_wq_target[hi])
                else:
                    slot = ci % N_RING
                    dst = ring[:, slot * 4096 + part * 2048: slot * 4096 + (part + 1) * 2048]
                    if part == 0:
                        def bl(e, dst=dst, lsl=lsl):
                            return e.copy(dst, lsl)
                        gstep("scalar", bl, wq=half_wq_target[hi])
                    else:
                        def bl(e, dst=dst, lsl=lsl):
                            return e.tensor_copy(dst, lsl)
                        gstep("vector", bl, wq=half_wq_target[hi])
                land_free_g[hi] = C["g"]
                issue_half(hi + N_LAND)
            chunk_cast_g[ci] = C["g"]

        def ring_slot(ci):
            return ring[:, (ci % N_RING) * 4096:((ci % N_RING) + 1) * 4096]

        def xn_v(c, t0, t1):
            return xn_fm[:, c * 1024 + t0: c * 1024 + t1]

        def attn_v(c, t0, t1):
            return attn_bf[:, c * 1024 + t0: c * 1024 + t1]

        init_loads = [
            (biases[:, 0:32], din["bq"]), (biases[:, 32:64], din["bo"]),
            (biases[:, 64:96], din["bfb"]), (biases[:, 96:104], din["bk"]),
            (biases[:, 104:112], din["bv"]), (biases[:, 112:114], din["creps"]),
            (biases[:, 114:116], din["sreps"]), (ident_f[:], din["ident"]),
        ]
        for dst, src in init_loads:
            C["xq"] += 16
            def fn(e, dst=dst, src=src):
                e.dma_start(out=dst, in_=src).then_inc(Sxq, 16)
            em("scalar", fn)
        init_xq = C["xq"]

        xt_xq_target = {}

        def issue_xtok(i, gv):
            src = din["x_tok"].rearrange("(n p) d -> n p d", p=128)[i]
            dst = xt_sl[i % 2]
            C["xq"] += 16
            xt_xq_target[i] = C["xq"]
            def fn(e, src=src, dst=dst, gv=gv):
                if gv:
                    e.wait_ge(Sg, gv)
                e.dma_start(out=dst, in_=src).then_inc(Sxq, 16)
            em("scalar", fn)

        issue_xtok(0, 0)
        issue_xtok(1, 0)

        gstep("vector", lambda e: e.tensor_copy(ident_b[:], ident_f[:]),
              xq=init_xq)
        gstep("vector", lambda e: e.tensor_mul(w128_c, creps_c, sreps_c))
        gstep("vector", lambda e: e.memset(ones_c[:], 1.0))

        for _h in range(N_LAND):
            issue_half(_h)

        for i in range(NTT):
            xt = xt_sl[i % 2]
            gstep("vector", lambda e, xt=xt: e.reduce_sum(
                stats[:, 0:1], xt, axis=AX), xq=xt_xq_target[i])
            gstep("vector", lambda e, xt=xt: e.scalar_tensor_tensor(
                xn_tm, xt, 1.0, xt, ALU.mult, ALU.mult,
                accum_out=stats[:, 1:2]))
            gstep("vector", lambda e: e.tensor_scalar_mul(
                stats[:, 2:3], stats[:, 0:1], 1.0 / D))
            gstep("vector", lambda e: e.tensor_mul(
                stats[:, 3:4], stats[:, 2:3], stats[:, 2:3]))
            gstep("vector", lambda e: e.scalar_tensor_tensor(
                stats[:, 4:5], stats[:, 1:2], 1.0 / D, stats[:, 3:4],
                ALU.mult, ALU.subtract))
            gstep("vector", lambda e: e.tensor_scalar_add(
                stats[:, 4:5], stats[:, 4:5], LN_EPS))
            gstep("scalar", lambda e: e.activation(
                stats[:, 5:6], stats[:, 4:5], AF.Ln))
            gstep("scalar", lambda e: e.activation(
                stats[:, 6:7], stats[:, 5:6], AF.Exp, bias=0.0, scale=-0.5))
            gstep("vector", lambda e: e.scalar_tensor_tensor(
                stats[:, 7:8], stats[:, 2:3], -1.0, stats[:, 6:7],
                ALU.mult, ALU.mult))
            gstep("scalar", lambda e, xt=xt: e.activation(
                xn_tm, xt, AF.Identity, bias=stats[:, 7:8],
                scale=stats[:, 6:7]))
            for b4 in range(4):
                def trb(e, b4=b4):
                    last = None
                    for c8 in range(8):
                        c = b4 * 8 + c8
                        last = e.transpose(
                            PT[:, c8 * 128:(c8 + 1) * 128],
                            xn_tm[:, c * 128:(c + 1) * 128], ident_b[:])
                    return last
                gstep("tensor", trb)
                c0 = b4 * 8
                def ev(e, c0=c0, i=i):
                    src = PT[:].rearrange("p (c t) -> p c t", t=128)
                    dst = xn_fm[:].rearrange("p (c t) -> p c t", t=1024)[
                        :, c0:c0 + 8, i * 128:(i + 1) * 128]
                    return e.tensor_copy(dst, src)
                gstep("vector", ev)
            if i + 2 < NTT:
                issue_xtok(i + 2, C["g"])

        for oc in range(HKV):
            ck, cv = 2 * oc, 2 * oc + 1
            cast_chunk(ck)
            cast_chunk(cv)
            wk_sl, wv_sl = ring_slot(ck), ring_slot(cv)
            for kv, wsl, fmdst, phi in ((0, wk_sl, k_fm, True),
                                        (1, wv_sl, v_fm, False)):
                for th in range(2):
                    PP = P0 if th == 0 else P1
                    def mmk(e, wsl=wsl, PP=PP, th=th):
                        last = None
                        for c in range(NCF):
                            last = e.matmul(
                                PP[:], wsl[:, c * 128:(c + 1) * 128],
                                xn_v(c, th * 512, (th + 1) * 512),
                                start=(c == 0), stop=(c == NCF - 1))
                        return last
                    gstep("tensor", mmk)
                    bcol = (bk_c if kv == 0 else bv_c)[:, oc:oc + 1]
                    if phi:
                        gstep("vector", lambda e, PP=PP, bcol=bcol: e.tensor_scalar(
                            e_scr[:], PP[:], bcol, 0.0, ALU.add, ALU.min))
                        gstep("scalar", lambda e: e.activation(
                            e_scr[:], e_scr[:], AF.Exp))
                        gstep("vector", lambda e, PP=PP, bcol=bcol: e.tensor_scalar(
                            r_scr[:], PP[:], bcol, 0.0, ALU.add, ALU.max))
                        gstep("vector", lambda e, fmdst=fmdst, th=th: e.tensor_add(
                            fmdst[:, th * 512:(th + 1) * 512], e_scr[:], r_scr[:]))
                    else:
                        gstep("vector", lambda e, PP=PP, bcol=bcol, fmdst=fmdst, th=th:
                              e.tensor_scalar_add(
                                  fmdst[:, th * 512:(th + 1) * 512], PP[:], bcol))
                tmdst = k_tm if kv == 0 else v_tm
                def trk(e, fmdst=fmdst):
                    last = None
                    for b in range(8):
                        last = e.transpose(
                            PT[:, (b % 8) * 128:((b % 8) + 1) * 128],
                            fmdst[:, b * 128:(b + 1) * 128], ident_b[:])
                    return last
                gstep("tensor", trk)
                gstep("vector", lambda e, tmdst=tmdst: e.tensor_copy(
                    tmdst[:], PT[:]))
            def kvtmm(e):
                last = None
                for b in range(8):
                    last = e.matmul(
                        P2[:, 0:128], k_tm[:, b * 128:(b + 1) * 128],
                        v_tm[:, b * 128:(b + 1) * 128],
                        start=(b == 0), stop=(b == 7))
                return last
            gstep("tensor", kvtmm)
            def normm(e):
                last = None
                for b in range(8):
                    last = e.matmul(
                        P2[:, 128:129], k_tm[:, b * 128:(b + 1) * 128],
                        ones_c[:], start=(b == 0), stop=(b == 7))
                return last
            gstep("tensor", normm)
            gstep("vector", lambda e, oc=oc: e.tensor_copy(
                staging[:, oc * 129:oc * 129 + 128], P2[:, 0:128]))
            gstep("vector", lambda e, oc=oc: e.tensor_copy(
                staging[:, oc * 129 + 128:oc * 129 + 129], P2[:, 128:129]))

        C["xq"] += 16
        ccin_tgt = C["xq"]
        gv_stage = C["g"]
        def fn_ccin(e, gv=gv_stage):
            e.wait_ge(Sg, gv)
            e.dma_start(out=cc_in, in_=staging[:]).then_inc(Sxq, 16)
        em("scalar", fn_ccin)

        def fn_cc(e):
            e.wait_ge(Sxq, ccin_tgt)
            e.collective_compute(
                "AllReduce", ALU.add,
                replica_groups=[[0, 1, 2, 3], [4, 5, 6, 7]],
                ins=[cc_in], outs=[cc_out],
            ).then_inc(Scc, 1)
        em("gpsimd", fn_cc)

        C["xq"] += 16
        ccout_tgt = C["xq"]
        def fn_ccout(e):
            e.wait_ge(Scc, 1)
            e.dma_start(out=staging[:], in_=cc_out).then_inc(Sxq, 16)
        em("scalar", fn_ccout)

        stag_v = staging[:].rearrange("p (h c) -> p h c", c=129)
        kvt_v = kvt_bf[:].rearrange("p (h c) -> p h c", c=128)
        gstep("vector", lambda e: e.tensor_copy(
            kvt_v, stag_v[:, :, 0:128]), xq=ccout_tgt)
        nrm_view = stag_v[:, :, 128:129]
        gstep("vector", lambda e: e.tensor_scalar_add(
            stats[:, 8:16].rearrange("p (h o) -> p h o", o=1),
            nrm_view, ATT_EPS))
        gstep("vector", lambda e: e.reciprocal(
            stats[:, 16:24], stats[:, 8:16]))

        for j in range(HQ):
            cq = 16 + j
            cast_chunk(cq)
            wq_sl = ring_slot(cq)
            gcol = j // 4
            for th in range(2):
                PP = P0 if th == 0 else P1
                PA = P3 if th == 0 else P4
                def mmq(e, wq_sl=wq_sl, PP=PP, th=th):
                    last = None
                    for c in range(NCF):
                        last = e.matmul(
                            PP[:], wq_sl[:, c * 128:(c + 1) * 128],
                            xn_v(c, th * 512, (th + 1) * 512),
                            start=(c == 0), stop=(c == NCF - 1))
                    return last
                gstep("tensor", mmq)
                bcol = bq_c[:, j:j + 1]
                qsl = q_bf[:, th * 512:(th + 1) * 512]
                gstep("vector", lambda e, PP=PP, bcol=bcol: e.tensor_scalar(
                    e_scr[:], PP[:], bcol, 0.0, ALU.add, ALU.min))
                gstep("scalar", lambda e: e.activation(
                    e_scr[:], e_scr[:], AF.Exp))
                gstep("vector", lambda e, PP=PP, bcol=bcol: e.tensor_scalar(
                    r_scr[:], PP[:], bcol, 0.0, ALU.add, ALU.max))
                gstep("vector", lambda e, qsl=qsl: e.tensor_add(
                    qsl, e_scr[:], r_scr[:]))
                gstep("tensor", lambda e, PA=PA, qsl=qsl, gcol=gcol: e.matmul(
                    PA[:], kvt_bf[:, gcol * 128:(gcol + 1) * 128], qsl,
                    start=True, stop=True))
                gstep("vector", lambda e, PA=PA, j=j, th=th, gcol=gcol:
                      e.tensor_scalar_mul(
                          attn_v(j, th * 512, (th + 1) * 512), PA[:],
                          stats[:, 16 + gcol:17 + gcol]))

        for gi in range(2):
            ca = 48 + gi
            cast_chunk(ca)
            a_sl = ring_slot(ca)
            for th in range(2):
                PP = P0 if th == 0 else P1
                def mma(e, a_sl=a_sl, PP=PP, th=th):
                    last = None
                    for c in range(NCF):
                        last = e.matmul(
                            PP[:], a_sl[:, c * 128:(c + 1) * 128],
                            xn_v(c, th * 512, (th + 1) * 512),
                            start=(c == 0), stop=(c == NCF - 1))
                    return last
                gstep("tensor", mma)
                gstep("vector", lambda e, PP=PP, gi=gi, th=th: e.tensor_copy(
                    xa_bf[:, gi * 1024 + th * 512: gi * 1024 + (th + 1) * 512],
                    PP[:]))
        for gi in range(2):
            cb = 50 + gi
            cast_chunk(cb)

        xfm_r = din["x_fm"].rearrange("(r p) t -> r p t", p=128)
        out_r = out_fm.rearrange("(r p) t -> r p t", p=128)
        xfm_xq_target = {}

        def issue_xfm(k, gv):
            j, th = k // 2, k % 2
            src = xfm_r[j][:, th * 512:(th + 1) * 512]
            dst = x_scr[:, (k % 3) * 512:((k % 3) + 1) * 512]
            C["xq"] += 16
            xfm_xq_target[k] = C["xq"]
            def fn(e, src=src, dst=dst, gv=gv):
                if gv:
                    e.wait_ge(Sg, gv)
                e.dma_start(out=dst, in_=src).then_inc(Sxq, 16)
            em("scalar", fn)

        for k in range(3):
            issue_xfm(k, C["g"])

        for j in range(32):
            co, cf = 52 + 2 * j, 53 + 2 * j
            cast_chunk(co)
            cast_chunk(cf)
            wo_sl, wf_sl = ring_slot(co), ring_slot(cf)
            for th in range(2):
                k = 2 * j + th
                PS = P0 if th == 0 else P1
                PG = P3 if th == 0 else P4
                def mms(e, wo_sl=wo_sl, PS=PS, th=th):
                    last = None
                    for c in range(NCF):
                        last = e.matmul(
                            PS[:], wo_sl[:, c * 128:(c + 1) * 128],
                            attn_v(c, th * 512, (th + 1) * 512),
                            start=(c == 0), stop=(c == NCF - 1))
                    return last
                gstep("tensor", mms)
                def mmg(e, wf_sl=wf_sl, PG=PG, th=th, j=j):
                    last = None
                    for c in range(NCF):
                        last = e.matmul(
                            PG[:], wf_sl[:, c * 128:(c + 1) * 128],
                            xn_v(c, th * 512, (th + 1) * 512),
                            start=(c == 0), stop=False)
                    for gi in range(2):
                        last = e.matmul(
                            PG[:], bft_w[:, gi * 4096 + j * 128: gi * 4096 + (j + 1) * 128],
                            xa_bf[:, gi * 1024 + th * 512: gi * 1024 + (th + 1) * 512],
                            start=False, stop=(gi == 1))
                    return last
                gstep("tensor", mmg)
                gstep("vector", lambda e, PS=PS, j=j: e.tensor_scalar_add(
                    s_scr[:], PS[:], bo_c[:, j:j + 1]))
                gstep("scalar", lambda e, PG=PG, j=j: e.activation(
                    g_scr[:], PG[:], AF.Silu, bias=bfb_c[:, j:j + 1], scale=1.0))
                gstep("vector", lambda e: e.tensor_mul(
                    g_scr[:], s_scr[:], g_scr[:]))
                osl = o_scr[:, (k % 2) * 512:((k % 2) + 1) * 512]
                od_need = 16 * (k - 1) if k >= 2 else 0
                gstep("vector", lambda e, osl=osl, k=k: e.tensor_add(
                    osl, g_scr[:], x_scr[:, (k % 3) * 512:((k % 3) + 1) * 512]),
                    xq=xfm_xq_target[k], od=od_need)
                C["od"] += 16
                gv_store = C["g"]
                def fst(e, j=j, th=th, osl=osl, gv=gv_store):
                    e.wait_ge(Sg, gv)
                    e.dma_start(out=out_r[j][:, th * 512:(th + 1) * 512],
                                in_=osl).then_inc(Sod, 16)
                em("scalar", fst)
                if k + 3 < 64:
                    issue_xfm(k + 3, C["g"])

        def fin(e):
            e.wait_ge(Sod, 64 * 16)
        em("scalar", fin)

        @block.sync
        def _(eng):
            for fn in streams["sync"]:
                fn(eng)

        @block.tensor
        def _(eng):
            for fn in streams["tensor"]:
                fn(eng)

        @block.vector
        def _(eng):
            for fn in streams["vector"]:
                fn(eng)

        @block.scalar
        def _(eng):
            for fn in streams["scalar"]:
                fn(eng)

        @block.gpsimd
        def _(eng):
            for fn in streams["gpsimd"]:
                fn(eng)

    return nc


def _pack_w(w):
    O, F = w.shape
    return np.ascontiguousarray(
        w.reshape(O // 128, 128, F // 128, 128).transpose(0, 3, 2, 1)
        .reshape(O // 128, 128, F))


def kernel(L0_attn_hidden_states, L0_attn_Wq, L0_attn_bq, L0_attn_Wk,
           L0_attn_bk, L0_attn_Wv, L0_attn_bv, L0_attn_Wo, L0_attn_bo,
           L0_ffn_W, L0_ffn_bias, L0_ffn_A, L0_ffn_Bf, L0_ffn_coeffs,
           L0_ffn_scales):
    x = np.ascontiguousarray(np.asarray(L0_attn_hidden_states, dtype=np.float32))
    wq = _pack_w(np.asarray(L0_attn_Wq, np.float32))
    wk = _pack_w(np.asarray(L0_attn_Wk, np.float32))
    wv = _pack_w(np.asarray(L0_attn_Wv, np.float32))
    wo = _pack_w(np.asarray(L0_attn_Wo, np.float32))
    wf = _pack_w(np.asarray(L0_ffn_W, np.float32))
    A = np.asarray(L0_ffn_A, np.float32)
    Bf = np.asarray(L0_ffn_Bf, np.float32)
    apack = np.ascontiguousarray(
        A.reshape(2, 8, 16, 32, 128)
        .transpose(0, 4, 3, 1, 2)
        .reshape(2, 128, 4096))
    bft = np.ascontiguousarray(
        Bf.transpose(0, 2, 1).reshape(2, 128, 4096))

    def bias_cols(b, n):
        return np.ascontiguousarray(np.asarray(b, np.float32).reshape(n, 128).T)

    bq = bias_cols(L0_attn_bq, 32)
    bk = bias_cols(L0_attn_bk, 8)
    bv = bias_cols(L0_attn_bv, 8)
    bo = bias_cols(L0_attn_bo, 32)
    bfb = bias_cols(L0_ffn_bias, 32)
    coeffs = np.asarray(L0_ffn_coeffs, np.float32)
    scales = np.asarray(L0_ffn_scales, np.float32)
    creps = np.ascontiguousarray(
        np.repeat(coeffs.reshape(2, 8), 16, axis=1).reshape(2, 128).T)
    sreps = np.ascontiguousarray(
        np.repeat(scales.reshape(2, 8), 16, axis=1).reshape(2, 128).T)
    ident = np.eye(128, dtype=np.float32)

    x_tok_all = x.reshape(2, 4, 1024, 4096)
    x_fm_all = np.ascontiguousarray(x.transpose(2, 0, 1).reshape(4096, 8192))

    shared = {
        "wq": wq, "wk": wk, "wv": wv, "wo": wo, "wf": wf,
        "apack": apack, "bft": bft, "bq": bq, "bk": bk, "bv": bv,
        "bo": bo, "bfb": bfb, "creps": creps, "sreps": sreps, "ident": ident,
    }
    in_maps = []
    for core in range(8):
        b, c = core // 4, core % 4
        m = dict(shared)
        m["x_tok"] = np.ascontiguousarray(x_tok_all[b, c])
        m["x_fm"] = np.ascontiguousarray(
            x_fm_all[:, (b * 4 + c) * 1024:(b * 4 + c + 1) * 1024])
        in_maps.append(m)

    nc = build_nc()
    res = run_bass_kernel_spmd(nc, in_maps, core_ids=list(range(8)))
    out_fm_full = np.concatenate(
        [res.results[core]["out_fm"] for core in range(8)], axis=1)
    out = out_fm_full.reshape(4096, 2, 4096).transpose(1, 2, 0)
    return np.ascontiguousarray(out)


# version 05
# speedup vs baseline: 1.3417x; 1.3417x over previous
import numpy as np
from contextlib import ExitStack

import concourse.bass as bass
import concourse.mybir as mybir
from concourse.bass_utils import run_bass_kernel_spmd

BUILD_LOG = []

F32 = mybir.dt.float32
BF16 = mybir.dt.bfloat16
U8 = mybir.dt.uint8
AF = mybir.ActivationFunctionType
ALU = mybir.AluOpType
AX = mybir.AxisListType.X

D = 4096
NCF = 32
NT = 1024
NTT = 8
HQ = 32
HKV = 8
LN_EPS = 1e-5
ATT_EPS = 1e-10

CHUNKS = []
for oc in range(HKV):
    CHUNKS.append(("wk", oc))
    CHUNKS.append(("wv", oc))
CHUNKS.append(("apack", 0))
CHUNKS.append(("apack", 1))
CHUNKS.append(("bft", 0))
CHUNKS.append(("bft", 1))
for j in range(HQ):
    CHUNKS.append(("wq", j))
for j in range(32):
    CHUNKS.append(("wo", j))
    CHUNKS.append(("wf", j))
NCHUNK = len(CHUNKS)

N_LAND = 2
N_RING = 2


def build_nc():
    nc = bass.Bass("TRN2", target_bir_lowering=False, debug=False)

    din = {}
    for name, shape in [
        ("x_tok", [NT, D]), ("x_fm", [D, NT]),
        ("wq", [32, 128, 4096]), ("wk", [8, 128, 4096]), ("wv", [8, 128, 4096]),
        ("wo", [32, 128, 4096]), ("wf", [32, 128, 4096]),
        ("apack", [2, 128, 4096]), ("bft", [2, 128, 4096]),
        ("bq", [128, 32]), ("bk", [128, 8]), ("bv", [128, 8]),
        ("bo", [128, 32]), ("bfb", [128, 32]),
        ("creps", [128, 2]), ("sreps", [128, 2]),
        ("ident", [128, 128]),
    ]:
        din[name] = nc.dram_tensor(name, shape, F32, kind="ExternalInput").ap()
    out_fm = nc.dram_tensor("out_fm", [D, NT], F32, kind="ExternalOutput").ap()
    cc_in = nc.dram_tensor("cc_in", [128, 1032], F32).ap()
    cc_out = nc.dram_tensor("cc_out", [128, 1032], F32).ap()

    with ExitStack() as ctx:
        def sbt(name, shape, dt):
            return ctx.enter_context(nc.sbuf_tensor(name, shape, dt))

        def pst(name, shape, dt):
            return ctx.enter_context(nc.psum_tensor(name, shape, dt))

        xn_fm = sbt("xn_fm", [128, NCF * 1024], BF16)
        ring = sbt("ring", [128, N_RING * 4096], BF16)
        land = sbt("land", [128, N_LAND * 2048], F32)
        bft_w = sbt("bft_w", [128, 2 * 4096], BF16)
        kvt_bf = sbt("kvt_bf", [128, 1024], BF16)
        stats = sbt("stats", [128, 32], F32)
        biases = sbt("biases", [128, 118], F32)
        ident_f = sbt("ident_f", [128, 128], F32)
        ident_b = sbt("ident_b", [128, 128], BF16)
        ones_c = sbt("ones_c", [128, 1], BF16)
        q_bf = sbt("q_bf", [128, 2 * 512], BF16)
        xa_bf = sbt("xa_bf", [128, 2 * 1024], BF16)
        e_scr = sbt("e_scr", [128, 2 * 512], F32)
        r_scr = sbt("r_scr", [128, 512], F32)
        g_scr = sbt("g_scr", [128, 2 * 512], F32)
        staging = sbt("staging", [128, 1032], F32)
        x_scr = sbt("x_scr", [128, 2 * 512], F32)
        o_scr = sbt("o_scr", [128, 2 * 512], F32)
        arena = sbt("arena", [128, 65536], U8)

        xt_sl = [arena[:, 0:16384].bitcast(F32), arena[:, 16384:32768].bitcast(F32)]
        xn_tm = arena[:, 32768:40960].bitcast(BF16)
        k_fm = arena[:, 40960:43008].bitcast(BF16)
        v_fm = arena[:, 43008:45056].bitcast(BF16)
        k_tm = arena[:, 45056:47104].bitcast(BF16)
        v_tm = arena[:, 47104:49152].bitcast(BF16)
        ARENA_P1 = ["xt0", "xt1", "xn_tm", "kfm", "vfm", "ktm", "vtm"]
        attn_bf = arena[:, 0:65536].bitcast(BF16)

        bq_c = biases[:, 0:32]
        bo_c = biases[:, 32:64]
        bfb_c = biases[:, 64:96]
        bk_c = biases[:, 96:104]
        bv_c = biases[:, 104:112]
        creps_c = biases[:, 112:114]
        sreps_c = biases[:, 114:116]
        w128_c = biases[:, 116:118]

        PT = pst("pt", [128, 1024], BF16)
        PT2 = pst("pt2", [128, 1024], BF16)
        P0 = pst("p0", [128, 512], F32)
        P1 = pst("p1", [128, 512], F32)
        P2 = pst("p2", [128, 512], F32)
        P3 = pst("p3", [128, 512], F32)
        PS_NAME = {id(PT): "PT", id(PT2): "PT2", id(P0): "P0", id(P1): "P1",
                   id(P2): "P2", id(P3): "P3"}

        Spe = ctx.enter_context(nc.semaphore("spe"))
        Sdv = ctx.enter_context(nc.semaphore("sdv"))
        Sac = ctx.enter_context(nc.semaphore("sac"))
        Spo = ctx.enter_context(nc.semaphore("spo"))
        Scc = ctx.enter_context(nc.semaphore("scc"))
        NQ = 8
        QSEM = {
            "wq": [ctx.enter_context(nc.semaphore(f"swq{i}")) for i in range(NQ)],
            "xq": [ctx.enter_context(nc.semaphore(f"sxq{i}")) for i in range(NQ)],
            "od": [ctx.enter_context(nc.semaphore(f"sod{i}")) for i in range(NQ)],
        }
        QIDX = {"wq": 0, "xq": 0, "od": 0}

        SEMNAME = {id(Spe): "Spe", id(Sdv): "Sdv", id(Sac): "Sac",
                   id(Spo): "Spo", id(Scc): "Scc"}
        for qn, sl in QSEM.items():
            for i, s in enumerate(sl):
                SEMNAME[id(s)] = f"{qn}{i}"
        BUILD_LOG.clear()

        block = ctx.enter_context(nc.Block())

        streams = {"sync": [], "tensor": [], "vector": [], "scalar": [],
                   "gpsimd": []}
        ENG_SEM = {"tensor": Spe, "vector": Sdv, "scalar": Sac,
                   "gpsimd": Spo}
        cnt = {"tensor": 0, "vector": 0, "scalar": 0, "gpsimd": 0, "cc": 0}
        qcnt = {k: [0] * NQ for k in QSEM}
        TRK = {}

        def em(eng, fn):
            streams[eng].append(fn)

        def _add_wait(waits, tok, own_sem):
            if tok is None:
                return
            sem, val = tok
            k = id(sem)
            if k in waits:
                waits[k] = (sem, max(waits[k][1], val))
            else:
                waits[k] = (sem, val)

        def _gather(own_sem, reads, writes):
            waits = {}
            for r in reads:
                e = TRK.get(r)
                if e:
                    _add_wait(waits, e["w"], own_sem)
            for r in writes:
                e = TRK.get(r)
                if e:
                    _add_wait(waits, e["w"], own_sem)
                    for tok in e["rd"].values():
                        _add_wait(waits, tok, own_sem)
            return waits

        def _commit(tok, reads, writes):
            for r in writes:
                TRK[r] = {"w": tok, "rd": {}}
            for r in reads:
                e = TRK.setdefault(r, {"w": None, "rd": {}})
                k = id(tok[0])
                old = e["rd"].get(k)
                if old is None or tok[1] > old[1]:
                    e["rd"][k] = tok

        def step(eng, builder, reads=(), writes=(), extra=(), inc=1):
            own = ENG_SEM[eng]
            waits = _gather(own, reads, writes)
            for tok in extra:
                _add_wait(waits, tok, None)
            wl = list(waits.values())
            cnt[eng] += inc
            tok = (own, cnt[eng])
            BUILD_LOG.append((eng, f"step r={reads} w={writes}",
                              [(SEMNAME[id(s)], v) for s, v in wl],
                              [(SEMNAME[id(own)], cnt[eng])]))
            def fn(e, wl=wl, builder=builder, own=own, inc=inc):
                for s, v in wl:
                    e.wait_ge(s, v)
                inst = builder(e)
                inst.then_inc(own, inc)
            em(eng, fn)
            _commit(tok, reads, writes)
            return tok

        def dma(eng, qkey, out_ap, in_ap, reads=(), writes=()):
            waits = _gather(None, reads, writes)
            wl = list(waits.values())
            r = QIDX[qkey] % NQ
            QIDX[qkey] += 1
            qsem = QSEM[qkey][r]
            qcnt[qkey][r] += 16
            tok = (qsem, qcnt[qkey][r])
            BUILD_LOG.append((eng, f"dma r={reads} w={writes}",
                              [(SEMNAME[id(s)], v) for s, v in wl],
                              [(SEMNAME[id(qsem)], qcnt[qkey][r])]))
            def fn(e, wl=wl, out_ap=out_ap, in_ap=in_ap, qsem=qsem):
                for s, v in wl:
                    e.wait_ge(s, v)
                e.dma_start(out=out_ap, in_=in_ap).then_inc(qsem, 16)
            em(eng, fn)
            _commit(tok, reads, writes)
            return tok

        issued_halves = [0]

        def chunk_src(ci):
            kind, idx = CHUNKS[ci]
            return din[kind][idx]

        def issue_half(hi):
            if hi >= NCHUNK * 2:
                return
            assert hi == issued_halves[0], (hi, issued_halves[0])
            ci, part = hi // 2, hi % 2
            src = chunk_src(ci)[:, part * 2048:(part + 1) * 2048]
            dst = land[:, (hi % N_LAND) * 2048:((hi % N_LAND) + 1) * 2048]
            dma("sync", "wq", dst, src, reads=(), writes=(f"land{hi % N_LAND}",))
            issued_halves[0] += 1

        def cast_chunk(ci):
            kind, idx = CHUNKS[ci]
            for part in range(2):
                hi = 2 * ci + part
                lres = f"land{hi % N_LAND}"
                lsl = land[:, (hi % N_LAND) * 2048:((hi % N_LAND) + 1) * 2048]
                if kind == "bft":
                    dst = bft_w[:, idx * 4096 + part * 2048: idx * 4096 + (part + 1) * 2048]
                    wcol = w128_c[:, idx:idx + 1]
                    step("vector", lambda e, dst=dst, lsl=lsl, wcol=wcol:
                         e.tensor_scalar_mul(dst, lsl, wcol),
                         reads=(lres, "biases"), writes=("bft_w",))
                else:
                    dres = f"ring{ci % N_RING}"
                    dst = ring[:, (ci % N_RING) * 4096 + part * 2048:
                               (ci % N_RING) * 4096 + (part + 1) * 2048]
                    if part == 0:
                        step("scalar", lambda e, dst=dst, lsl=lsl: e.copy(dst, lsl),
                             reads=(lres,), writes=(dres,))
                    else:
                        step("vector", lambda e, dst=dst, lsl=lsl: e.tensor_copy(dst, lsl),
                             reads=(lres,), writes=(dres,))
                issue_half(hi + N_LAND)

        def ring_slot(ci):
            return ring[:, (ci % N_RING) * 4096:((ci % N_RING) + 1) * 4096]

        def xn_v(c, t0, t1):
            return xn_fm[:, c * 1024 + t0: c * 1024 + t1]

        def attn_v(c, t0, t1):
            return attn_bf[:, c * 1024 + t0: c * 1024 + t1]

        for dst, src, res in [
            (biases[:, 0:32], din["bq"], "biases"),
            (biases[:, 32:64], din["bo"], "biases"),
            (biases[:, 64:96], din["bfb"], "biases"),
            (biases[:, 96:104], din["bk"], "biases"),
            (biases[:, 104:112], din["bv"], "biases"),
            (biases[:, 112:114], din["creps"], "biases"),
            (biases[:, 114:116], din["sreps"], "biases"),
            (ident_f[:], din["ident"], "ident_f"),
        ]:
            dma("scalar", "xq", dst, src, writes=(res,))

        def issue_xtok(i):
            src = din["x_tok"].rearrange("(n p) d -> n p d", p=128)[i]
            dma("scalar", "xq", xt_sl[i % 2], src,
                writes=(f"xt{i % 2}",))

        issue_xtok(0)
        issue_xtok(1)

        step("vector", lambda e: e.tensor_copy(ident_b[:], ident_f[:]),
             reads=("ident_f",), writes=("ident_b",))
        step("vector", lambda e: e.tensor_mul(w128_c, creps_c, sreps_c),
             reads=("biases",), writes=("biases",))
        step("vector", lambda e: e.memset(ones_c[:], 1.0), writes=("ones",))

        for h in range(N_LAND):
            issue_half(h)

        for i in range(NTT):
            xt = xt_sl[i % 2]
            xres = f"xt{i % 2}"
            step("vector", lambda e, xt=xt: e.reduce_sum(
                stats[:, 0:1], xt, axis=AX), reads=(xres,), writes=("stats",))
            step("scalar", lambda e, xt=xt: e.activation(
                xn_tm, xt, AF.Square, accum_out=stats[:, 1:2]),
                reads=(xres,), writes=("xn_tm", "stats"))
            step("vector", lambda e: e.tensor_scalar_mul(
                stats[:, 2:3], stats[:, 0:1], 1.0 / D),
                reads=("stats",), writes=("stats",))
            step("vector", lambda e: e.tensor_mul(
                stats[:, 3:4], stats[:, 2:3], stats[:, 2:3]),
                reads=("stats",), writes=("stats",))
            step("vector", lambda e: e.scalar_tensor_tensor(
                stats[:, 4:5], stats[:, 1:2], 1.0 / D, stats[:, 3:4],
                ALU.mult, ALU.subtract), reads=("stats",), writes=("stats",))
            step("vector", lambda e: e.tensor_scalar_add(
                stats[:, 4:5], stats[:, 4:5], LN_EPS),
                reads=("stats",), writes=("stats",))
            step("scalar", lambda e: e.activation(
                stats[:, 5:6], stats[:, 4:5], AF.Ln),
                reads=("stats",), writes=("stats",))
            step("scalar", lambda e: e.activation(
                stats[:, 6:7], stats[:, 5:6], AF.Exp, bias=0.0, scale=-0.5),
                reads=("stats",), writes=("stats",))
            step("vector", lambda e: e.scalar_tensor_tensor(
                stats[:, 7:8], stats[:, 2:3], -1.0, stats[:, 6:7],
                ALU.mult, ALU.mult), reads=("stats",), writes=("stats",))
            step("scalar", lambda e, xt=xt: e.activation(
                xn_tm, xt, AF.Identity, bias=stats[:, 7:8],
                scale=stats[:, 6:7]), reads=(xres, "stats"), writes=("xn_tm",))
            for b4 in range(4):
                PTx = PT if b4 % 2 == 0 else PT2
                pres = "PT" if b4 % 2 == 0 else "PT2"
                def trb(e, b4=b4, PTx=PTx):
                    last = None
                    for c8 in range(8):
                        c = b4 * 8 + c8
                        last = e.transpose(
                            PTx[:, c8 * 128:(c8 + 1) * 128],
                            xn_tm[:, c * 128:(c + 1) * 128], ident_b[:])
                    return last
                step("tensor", trb, reads=("xn_tm", "ident_b"), writes=(pres,))
                c0 = b4 * 8
                def ev(e, c0=c0, i=i, PTx=PTx):
                    src = PTx[:].rearrange("p (c t) -> p c t", t=128)
                    dst = xn_fm[:].rearrange("p (c t) -> p c t", t=1024)[
                        :, c0:c0 + 8, i * 128:(i + 1) * 128]
                    return e.tensor_copy(dst, src)
                step("vector", ev, reads=(pres,), writes=("xn_fm",))
            if i + 2 < NTT:
                issue_xtok(i + 2)

        cast_chunk(0)
        cast_chunk(1)
        for oc in range(HKV):
            ck, cv = 2 * oc, 2 * oc + 1
            wk_sl, wv_sl = ring_slot(ck), ring_slot(cv)
            rk, rv = f"ring{ck % N_RING}", f"ring{cv % N_RING}"
            for kv, wsl, wres, fmdst, fres, phi in (
                    (0, wk_sl, rk, k_fm, "kfm", True),
                    (1, wv_sl, rv, v_fm, "vfm", False)):
                for th in range(2):
                    PP = P0 if th == 0 else P1
                    ppres = "P0" if th == 0 else "P1"
                    def mmk(e, wsl=wsl, PP=PP, th=th):
                        last = None
                        for c in range(NCF):
                            last = e.matmul(
                                PP[:], wsl[:, c * 128:(c + 1) * 128],
                                xn_v(c, th * 512, (th + 1) * 512),
                                start=(c == 0), stop=(c == NCF - 1))
                        return last
                    step("tensor", mmk, reads=(wres, "xn_fm"), writes=(ppres,))
                    bcol = (bk_c if kv == 0 else bv_c)[:, oc:oc + 1]
                    esl = e_scr[:, th * 512:(th + 1) * 512]
                    rsl = r_scr[:]
                    eres, rres = f"e{th}", "r"
                    if phi:
                        step("vector", lambda e, PP=PP, bcol=bcol, esl=esl:
                             e.tensor_scalar(esl, PP[:], bcol, 0.0,
                                             ALU.add, ALU.min),
                             reads=(ppres, "biases"), writes=(eres,))
                        step("scalar", lambda e, esl=esl: e.activation(
                            esl, esl, AF.Exp), reads=(eres,), writes=(eres,))
                        step("vector", lambda e, PP=PP, bcol=bcol, rsl=rsl:
                             e.tensor_scalar(rsl, PP[:], bcol, 0.0,
                                             ALU.add, ALU.max),
                             reads=(ppres, "biases"), writes=(rres,))
                        step("vector", lambda e, fmdst=fmdst, th=th, esl=esl,
                             rsl=rsl: e.tensor_add(
                                 fmdst[:, th * 512:(th + 1) * 512], esl, rsl),
                             reads=(eres, rres), writes=(fres,))
                    else:
                        step("vector", lambda e, PP=PP, bcol=bcol, fmdst=fmdst,
                             th=th: e.tensor_scalar_add(
                                 fmdst[:, th * 512:(th + 1) * 512], PP[:], bcol),
                             reads=(ppres, "biases"), writes=(fres,))
                PTx = PT if kv == 0 else PT2
                pres = "PT" if kv == 0 else "PT2"
                tmdst, tres = (k_tm, "ktm") if kv == 0 else (v_tm, "vtm")
                def trk(e, fmdst=fmdst, PTx=PTx):
                    last = None
                    for b in range(8):
                        last = e.transpose(
                            PTx[:, b * 128:(b + 1) * 128],
                            fmdst[:, b * 128:(b + 1) * 128], ident_b[:])
                    return last
                step("tensor", trk, reads=(fres, "ident_b"), writes=(pres,))
                step("vector", lambda e, tmdst=tmdst, PTx=PTx: e.tensor_copy(
                    tmdst[:], PTx[:]), reads=(pres,), writes=(tres,))
            PK = P2 if oc % 2 == 0 else P3
            pkres = "P2" if oc % 2 == 0 else "P3"
            def kvtmm(e, PK=PK):
                last = None
                for b in range(8):
                    last = e.matmul(
                        PK[:, 0:128], k_tm[:, b * 128:(b + 1) * 128],
                        v_tm[:, b * 128:(b + 1) * 128],
                        start=(b == 0), stop=(b == 7))
                return last
            step("tensor", kvtmm, reads=("ktm", "vtm"), writes=(pkres,))
            def normm(e, PK=PK):
                last = None
                for b in range(8):
                    last = e.matmul(
                        PK[:, 128:129], k_tm[:, b * 128:(b + 1) * 128],
                        ones_c[:], start=(b == 0), stop=(b == 7))
                return last
            step("tensor", normm, reads=("ktm", "ones"), writes=(pkres,))
            def kvev(e, oc=oc, PK=PK):
                return e.tensor_copy(
                    staging[:, oc * 129:oc * 129 + 129], PK[:, 0:129])
            step("vector", kvev, reads=(pkres,), writes=("staging",))
            if oc + 1 < HKV:
                cast_chunk(2 * oc + 2)
                cast_chunk(2 * oc + 3)

        dma("scalar", "xq", cc_in, staging[:],
            reads=("staging",), writes=("cc_in",))
        cc_waits = list(_gather(None, ("cc_in",), ("cc_out",)).values())
        cnt["cc"] += 1
        cc_tok = (Scc, cnt["cc"])
        BUILD_LOG.append(("gpsimd", "collective",
                          [(SEMNAME[id(s)], v) for s, v in cc_waits],
                          [("Scc", cnt["cc"])]))
        def fn_cc(e, wl=cc_waits):
            for s, v in wl:
                e.wait_ge(s, v)
            e.collective_compute(
                "AllReduce", ALU.add,
                replica_groups=[[0, 1, 2, 3], [4, 5, 6, 7]],
                ins=[cc_in], outs=[cc_out],
            ).then_inc(Scc, 1)
        em("gpsimd", fn_cc)
        _commit(cc_tok, ("cc_in",), ("cc_out",))


        cast_chunk(16)
        for gi in range(2):
            ca = 16 + gi
            if gi == 0:
                cast_chunk(17)
            else:
                cast_chunk(18)
                cast_chunk(19)
            a_sl = ring_slot(ca)
            ares = f"ring{ca % N_RING}"
            for th in range(2):
                PP = P0 if th == 0 else P1
                ppres = "P0" if th == 0 else "P1"
                def mma(e, a_sl=a_sl, PP=PP, th=th):
                    last = None
                    for c in range(NCF):
                        last = e.matmul(
                            PP[:], a_sl[:, c * 128:(c + 1) * 128],
                            xn_v(c, th * 512, (th + 1) * 512),
                            start=(c == 0), stop=(c == NCF - 1))
                    return last
                step("tensor", mma, reads=(ares, "xn_fm"), writes=(ppres,))
                step("vector", lambda e, PP=PP, gi=gi, th=th: e.tensor_copy(
                    xa_bf[:, gi * 1024 + th * 512: gi * 1024 + (th + 1) * 512],
                    PP[:]), reads=(ppres,), writes=("xa",))

        dma("scalar", "xq", staging[:], cc_out,
            reads=("cc_out",), writes=("staging",))
        stag_v = staging[:].rearrange("p (h c) -> p h c", c=129)
        kvt_v = kvt_bf[:].rearrange("p (h c) -> p h c", c=128)
        step("vector", lambda e: e.tensor_copy(kvt_v, stag_v[:, :, 0:128]),
             reads=("staging",), writes=("kvt",))
        step("vector", lambda e: e.tensor_scalar_add(
            stats[:, 8:16].rearrange("p (h o) -> p h o", o=1),
            stag_v[:, :, 128:129], ATT_EPS),
            reads=("staging",), writes=("stats",))
        step("vector", lambda e: e.reciprocal(stats[:, 16:24], stats[:, 8:16]),
             reads=("stats",), writes=("stats",),
             extra=((Sdv, cnt["vector"]),))

        cast_chunk(20)
        first_attn = True
        for j in range(HQ):
            cq = 20 + j
            if j + 1 < HQ:
                cast_chunk(cq + 1)
            wq_sl = ring_slot(cq)
            wres = f"ring{cq % N_RING}"
            gcol = j // 4
            for th in range(2):
                PP = P0 if th == 0 else P1
                ppres = "P0" if th == 0 else "P1"
                PA = P2 if th == 0 else P3
                pares = "P2" if th == 0 else "P3"
                def mmq(e, wq_sl=wq_sl, PP=PP, th=th):
                    last = None
                    for c in range(NCF):
                        last = e.matmul(
                            PP[:], wq_sl[:, c * 128:(c + 1) * 128],
                            xn_v(c, th * 512, (th + 1) * 512),
                            start=(c == 0), stop=(c == NCF - 1))
                    return last
                step("tensor", mmq, reads=(wres, "xn_fm"), writes=(ppres,))
                bcol = bq_c[:, j:j + 1]
                esl = e_scr[:, th * 512:(th + 1) * 512]
                rsl = r_scr[:]
                eres, rres = f"e{th}", "r"
                qsl = q_bf[:, th * 512:(th + 1) * 512]
                qres = f"q{th}"
                step("vector", lambda e, PP=PP, bcol=bcol, esl=esl:
                     e.tensor_scalar(esl, PP[:], bcol, 0.0, ALU.add, ALU.min),
                     reads=(ppres, "biases"), writes=(eres,))
                step("scalar", lambda e, esl=esl: e.activation(
                    esl, esl, AF.Exp), reads=(eres,), writes=(eres,))
                step("vector", lambda e, PP=PP, bcol=bcol, rsl=rsl:
                     e.tensor_scalar(rsl, PP[:], bcol, 0.0, ALU.add, ALU.max),
                     reads=(ppres, "biases"), writes=(rres,))
                step("vector", lambda e, qsl=qsl, esl=esl, rsl=rsl:
                     e.tensor_add(qsl, esl, rsl),
                     reads=(eres, rres), writes=(qres,))
                step("tensor", lambda e, PA=PA, qsl=qsl, gcol=gcol: e.matmul(
                    PA[:], kvt_bf[:, gcol * 128:(gcol + 1) * 128], qsl,
                    start=True, stop=True),
                    reads=("kvt", qres), writes=(pares,))
                aw = ("attn",) + tuple(ARENA_P1) if first_attn else ("attn",)
                first_attn = False
                step("vector", lambda e, PA=PA, j=j, th=th, gcol=gcol:
                     e.tensor_scalar_mul(
                         attn_v(j, th * 512, (th + 1) * 512), PA[:],
                         stats[:, 16 + gcol:17 + gcol]),
                     reads=(pares, "stats"), writes=aw)

        xfm_r = din["x_fm"].rearrange("(r p) t -> r p t", p=128)
        out_r = out_fm.rearrange("(r p) t -> r p t", p=128)

        def issue_xfm(k):
            j, th = k // 2, k % 2
            src = xfm_r[j][:, th * 512:(th + 1) * 512]
            dst = x_scr[:, (k % 2) * 512:((k % 2) + 1) * 512]
            dma("scalar", "xq", dst, src, writes=(f"x{k % 2}",))

        for k in range(2):
            issue_xfm(k)

        cast_chunk(52)
        cast_chunk(53)
        for j in range(32):
            co, cf = 52 + 2 * j, 53 + 2 * j
            wo_sl, wf_sl = ring_slot(co), ring_slot(cf)
            ro, rf = f"ring{co % N_RING}", f"ring{cf % N_RING}"
            for th in range(2):
                k = 2 * j + th
                PS = P0 if th == 0 else P1
                psres = "P0" if th == 0 else "P1"
                PG = P2 if th == 0 else P3
                pgres = "P2" if th == 0 else "P3"
                def mms(e, wo_sl=wo_sl, PS=PS, th=th):
                    last = None
                    for c in range(NCF):
                        last = e.matmul(
                            PS[:], wo_sl[:, c * 128:(c + 1) * 128],
                            attn_v(c, th * 512, (th + 1) * 512),
                            start=(c == 0), stop=(c == NCF - 1))
                    return last
                step("tensor", mms, reads=(ro, "attn"), writes=(psres,))
                def mmg(e, wf_sl=wf_sl, PG=PG, th=th, j=j):
                    last = None
                    for c in range(NCF):
                        last = e.matmul(
                            PG[:], wf_sl[:, c * 128:(c + 1) * 128],
                            xn_v(c, th * 512, (th + 1) * 512),
                            start=(c == 0), stop=False)
                    for gi in range(2):
                        last = e.matmul(
                            PG[:], bft_w[:, gi * 4096 + j * 128: gi * 4096 + (j + 1) * 128],
                            xa_bf[:, gi * 1024 + th * 512: gi * 1024 + (th + 1) * 512],
                            start=False, stop=(gi == 1))
                    return last
                step("tensor", mmg, reads=(rf, "xn_fm", "bft_w", "xa"),
                     writes=(pgres,))
                gsl = g_scr[:, th * 512:(th + 1) * 512]
                gres = f"g{th}"
                step("scalar", lambda e, PG=PG, j=j, gsl=gsl: e.activation(
                    gsl, PG[:], AF.Silu, bias=bfb_c[:, j:j + 1], scale=1.0),
                    reads=(pgres, "biases"), writes=(gres,))
                step("vector", lambda e, PS=PS, j=j, gsl=gsl:
                     e.scalar_tensor_tensor(gsl, PS[:], bo_c[:, j:j + 1], gsl,
                                            ALU.add, ALU.mult),
                     reads=(psres, "biases", gres), writes=(gres,))
                osl = o_scr[:, (k % 2) * 512:((k % 2) + 1) * 512]
                ores = f"o{k % 2}"
                xres = f"x{k % 2}"
                step("vector", lambda e, osl=osl, gsl=gsl, k=k: e.tensor_add(
                    osl, gsl, x_scr[:, (k % 2) * 512:((k % 2) + 1) * 512]),
                    reads=(gres, xres), writes=(ores,))
                dma("scalar", "od", out_r[j][:, th * 512:(th + 1) * 512],
                    osl, reads=(ores,))
                if k + 2 < 64:
                    issue_xfm(k + 2)
            if j + 1 < 32:
                cast_chunk(co + 2)
                cast_chunk(cf + 2)

        od_final = [(QSEM["od"][i], qcnt["od"][i]) for i in range(NQ)
                    if qcnt["od"][i] > 0]
        BUILD_LOG.append(("scalar", "final",
                          [(SEMNAME[id(s)], v) for s, v in od_final], []))
        def fin(e, fl=od_final):
            for s, v in fl:
                e.wait_ge(s, v)
        em("scalar", fin)

        @block.sync
        def _(eng):
            for fn in streams["sync"]:
                fn(eng)

        @block.tensor
        def _(eng):
            for fn in streams["tensor"]:
                fn(eng)

        @block.vector
        def _(eng):
            for fn in streams["vector"]:
                fn(eng)

        @block.scalar
        def _(eng):
            for fn in streams["scalar"]:
                fn(eng)

        @block.gpsimd
        def _(eng):
            for fn in streams["gpsimd"]:
                fn(eng)

    return nc


def _pack_w(w):
    O, F = w.shape
    return np.ascontiguousarray(
        w.reshape(O // 128, 128, F // 128, 128).transpose(0, 3, 2, 1)
        .reshape(O // 128, 128, F))


def kernel(L0_attn_hidden_states, L0_attn_Wq, L0_attn_bq, L0_attn_Wk,
           L0_attn_bk, L0_attn_Wv, L0_attn_bv, L0_attn_Wo, L0_attn_bo,
           L0_ffn_W, L0_ffn_bias, L0_ffn_A, L0_ffn_Bf, L0_ffn_coeffs,
           L0_ffn_scales):
    x = np.ascontiguousarray(np.asarray(L0_attn_hidden_states, dtype=np.float32))
    wq = _pack_w(np.asarray(L0_attn_Wq, np.float32))
    wk = _pack_w(np.asarray(L0_attn_Wk, np.float32))
    wv = _pack_w(np.asarray(L0_attn_Wv, np.float32))
    wo = _pack_w(np.asarray(L0_attn_Wo, np.float32))
    wf = _pack_w(np.asarray(L0_ffn_W, np.float32))
    A = np.asarray(L0_ffn_A, np.float32)
    Bf = np.asarray(L0_ffn_Bf, np.float32)
    apack = np.ascontiguousarray(
        A.reshape(2, 8, 16, 32, 128)
        .transpose(0, 4, 3, 1, 2)
        .reshape(2, 128, 4096))
    bft = np.ascontiguousarray(
        Bf.transpose(0, 2, 1).reshape(2, 128, 4096))

    def bias_cols(b, n):
        return np.ascontiguousarray(np.asarray(b, np.float32).reshape(n, 128).T)

    bq = bias_cols(L0_attn_bq, 32)
    bk = bias_cols(L0_attn_bk, 8)
    bv = bias_cols(L0_attn_bv, 8)
    bo = bias_cols(L0_attn_bo, 32)
    bfb = bias_cols(L0_ffn_bias, 32)
    coeffs = np.asarray(L0_ffn_coeffs, np.float32)
    scales = np.asarray(L0_ffn_scales, np.float32)
    creps = np.ascontiguousarray(
        np.repeat(coeffs.reshape(2, 8), 16, axis=1).reshape(2, 128).T)
    sreps = np.ascontiguousarray(
        np.repeat(scales.reshape(2, 8), 16, axis=1).reshape(2, 128).T)
    ident = np.eye(128, dtype=np.float32)

    x_tok_all = x.reshape(2, 4, 1024, 4096)
    x_fm_all = np.ascontiguousarray(x.transpose(2, 0, 1).reshape(4096, 8192))

    shared = {
        "wq": wq, "wk": wk, "wv": wv, "wo": wo, "wf": wf,
        "apack": apack, "bft": bft, "bq": bq, "bk": bk, "bv": bv,
        "bo": bo, "bfb": bfb, "creps": creps, "sreps": sreps, "ident": ident,
    }
    in_maps = []
    for core in range(8):
        b, c = core // 4, core % 4
        m = dict(shared)
        m["x_tok"] = np.ascontiguousarray(x_tok_all[b, c])
        m["x_fm"] = np.ascontiguousarray(
            x_fm_all[:, (b * 4 + c) * 1024:(b * 4 + c + 1) * 1024])
        in_maps.append(m)

    nc = build_nc()
    res = run_bass_kernel_spmd(nc, in_maps, core_ids=list(range(8)))
    out_fm_full = np.concatenate(
        [res.results[core]["out_fm"] for core in range(8)], axis=1)
    out = out_fm_full.reshape(4096, 2, 4096).transpose(1, 2, 0)
    return np.ascontiguousarray(out)
